# version 01 (final)
# baseline (speedup 1.0000x reference)
import numpy as np
import concourse.bass as bass
import concourse.mybir as mybir
from concourse.bass_utils import run_bass_kernel_spmd

F32 = mybir.dt.float32
I32 = mybir.dt.int32
AF = mybir.ActivationFunctionType
ALU = mybir.AluOpType
AX = mybir.AxisListType

T = 2048
C = 1024
NT = 16
NJ = 8
LAYERS = (0, 1, 2, 3)

ENGS = ("tensor", "vector", "scalar", "gpsimd", "sync")


class Reg:
    __slots__ = ("w", "r")

    def __init__(self):
        self.w = None
        self.r = []


NSLOT = 14


class Prog:
    def __init__(self):
        self.items = {e: [] for e in ENGS}
        self.cc = {e: 0 for e in ENGS}
        self.dc = {"sync": 0, "gpsimd": 0}
        self.waited = {e: {} for e in ENGS}

    def op(self, eng, fn, ins=(), outs=(), dma=False):
        waits = {}

        def need(key, val):
            if key is None:
                return
            if val <= self.waited[eng].get(key, 0):
                return
            if val > waits.get(key, 0):
                waits[key] = val

        for r in ins:
            if r.w:
                need(*r.w)
        for r in outs:
            if r.w:
                need(*r.w)
            for key, val in r.r:
                need(key, val)
        if dma:
            idx = self.dc[eng]
            self.dc[eng] = idx + 1
            slot, cyc = idx % NSLOT, idx // NSLOT
            mykey, myval = ("ds", eng, slot), (cyc + 1) * 16
            if cyc > 0:
                need(mykey, cyc * 16)
        else:
            idx = self.cc[eng]
            self.cc[eng] = idx + 1
            mykey, myval = ("c", eng), idx + 1
        for key, val in waits.items():
            self.waited[eng][key] = val
        self.items[eng].append((tuple(waits.items()), fn, dma, mykey))
        for r in outs:
            r.w = (mykey, myval)
            r.r = []
        for r in ins:
            r.r.append((mykey, myval))

    def replay(self, eng, E, sems):
        own_c = sems.get(("c", eng))
        for waits, fn, dma, mykey in self.items[eng]:
            for key, val in waits:
                E.wait_ge(sems[key], val)
            inst = fn(E)
            if dma:
                inst.then_inc(sems[mykey], 16)
            else:
                inst.then_inc(own_c, 1)


def build_program(layers=LAYERS, tap=None):
    nc = bass.Bass("TRN2", target_bir_lowering=False)
    PR = Prog()

    def hT(name, shape, kind="ExternalInput"):
        return nc.dram_tensor(name, shape, F32, kind=kind).ap()

    x_h = hT("x", [T, C])
    out_h = hT("out", [T, C], kind="ExternalOutput")
    W = {}
    for n, s in [
        ("l0_wq", [C, C]), ("l0_bq", [128, 8]), ("l0_wk", [C, C]), ("l0_bk", [128, 8]),
        ("l0_wv", [C, C]), ("l0_bv", [1, C]), ("l0_wo", [C, C]), ("l0_bo", [1, C]),
        ("l0_wg", [2816, C]), ("l0_wu", [2816, C]), ("l0_wd", [C, 2816]),
        ("l1_wq", [C, C]), ("l1_bq", [128, 8]), ("l1_wk", [C, C]), ("l1_bk", [128, 8]),
        ("l1_wv", [C, C]), ("l1_bv", [1, C]), ("l1_wo", [C, C]), ("l1_bo", [1, C]),
        ("l1_gam", [1, C]), ("l1_bet", [1, C]), ("l1_wf", [C, C]),
        ("l2_wq", [C, C]), ("l2_wk", [C, C]), ("l2_wv", [C, C]),
        ("l2_cqw", [128, 32]), ("l2_cqb", [128, 8]), ("l2_ckw", [128, 32]),
        ("l2_ckb", [128, 8]), ("l2_cvw", [128, 32]), ("l2_cvb", [128, 8]),
        ("l2_wa", [C, 8]), ("l2_ba", [8, 1]), ("l2_wb", [C, 8]), ("l2_bb", [8, 1]),
        ("l2_wg", [C, C]), ("l2_wo", [C, C]), ("l2_lng", [1, 128]), ("l2_lnb", [1, 128]),
        ("l2_mwg", [C, 8]), ("l2_we", [8, C, C]), ("l2_be", [8, C]),
        ("l3_wq", [C, C]), ("l3_wk", [C, C]), ("l3_wv", [C, C]), ("l3_wo", [C, C]),
        ("l3_bo", [1, C]), ("l3_rel", [1, 17]),
        ("l3_wg", [C, 4096]), ("l3_wu", [C, 4096]), ("l3_wd", [4096, C]),
    ]:
        W[n] = hT(n, s)

    def sT(name, shape):
        return hT(name, shape, kind="ExternalOutput" if tap == name else "Internal")

    xnt_s = sT("xnt_s", [NJ, 128, T])
    xqt_s = sT("xqt_s", [NJ, 128, T])
    qt_s = sT("qt_s", [NJ, 128, T])
    kt_s = sT("kt_s", [NJ, 128, T])
    v_s = sT("v_s", [16, T, 64])
    ot_s = sT("ot_s", [NJ, 128, T])
    qraw_s = sT("qraw_s", [NJ, 128, T])
    kraw_s = sT("kraw_s", [NJ, 128, T])
    vraw_s = sT("vraw_s", [NJ, 128, T])
    q2_s = sT("q2_s", [8, 16, 128, 128])
    k2_s = sT("k2_s", [8, 16, 128, 128])
    v2_s = sT("v2_s", [8, 16, 128, 128])
    g2_s = sT("g2_s", [16, 128, C])
    h3_s = sT("h3_s", [16, 128, 4096])

    from contextlib import ExitStack
    st = ExitStack()

    def sb(name, shape, dt=F32):
        return st.enter_context(nc.sbuf_tensor(name, shape, dt))

    x_sb = sb("x_sb", [128, NT * C])
    xc = sb("xc", [128, C])
    xnb = sb("xnb", [128, C])
    xqb = sb("xqb", [128, C])
    scrA = sb("scrA", [128, 4096])
    rhs0 = sb("rhs0", [128, C])
    rhs1 = sb("rhs1", [128, C])
    rh2 = sb("rh2", [128, 2048])
    acc0 = sb("acc0", [128, C])
    bvb = sb("bvb", [128, C])
    bob = sb("bob", [128, C])
    wvf = sb("wvf", [128, 4096])
    wk_a = sb("wk_a", [128, T])
    wk_b = sb("wk_b", [128, T])
    wk_c = sb("wk_c", [128, T + 4])
    rows = sb("rows", [128, C])
    pt0 = sb("pt0", [128, 512])
    pt1 = sb("pt1", [128, 512])
    obuf = sb("obuf", [128, 512])
    bcb = sb("bcb", [128, 512])
    vg0 = sb("vg0", [128, 16 * 65])
    stq = [sb("st%d" % i, [128, 128]) for i in range(23)]
    ident = sb("ident", [128, 128])
    onesf = sb("onesf", [128, 128])
    mLs = sb("mLs", [128, 128])
    mUs = sb("mUs", [128, 128])
    mUi = sb("mUi", [128, 128])
    d8t = sb("d8t", [128, 128])
    u8t = sb("u8t", [128, 128])
    relb = sb("relb", [128, 32])
    mb = sb("mb", [128, 16])
    sm1 = sb("sm1", [128, 128])
    sm2 = sb("sm2", [128, 128])
    sm3 = sb("sm3", [128, 128])
    bqc = sb("bqc", [128, 8])
    bkc = sb("bkc", [128, 8])
    scol0 = sb("scol0", [128, 72])
    scol1 = sb("scol1", [128, 72])
    ibuf = st.enter_context(nc.sbuf_tensor("ibuf", [128, C], I32))
    srw = rh2

    pb = [st.enter_context(nc.psum_tensor("pb%d" % i, [128, 512], F32))
          for i in range(8)]

    regs = {}

    def R(name):
        if name not in regs:
            regs[name] = Reg()
        return regs[name]

    RX = [R("x%d" % i) for i in range(NT)]
    RPB = [R("pb%d" % i) for i in range(8)]
    RST = [R("st%d" % i) for i in range(23)]
    SRWR = [R("srw_c"), R("srw_cp"), R("srw_nb"), R("srw_wx"),
            R("srw_g"), R("srw_gp"), R("srw_gc")]

    def AQ(lo, hi):
        return [R("scrAq%d" % q) for q in range(lo // 1024, (hi + 1023) // 1024)]

    def WQ(lo, hi):
        return [R("wvfq%d" % q) for q in range(lo // 1024, (hi + 1023) // 1024)]

    def mm(p, lhsT, rhs, startg, stopg, ins, outs):
        PR.op("tensor",
              lambda E, o=p, l=lhsT, r=rhs, s1=startg, s2=stopg:
              E.matmul(o, l, r, start=s1, stop=s2), ins, outs)

    def tr(p, inp, ins, outs):
        PR.op("tensor",
              lambda E, o=p, i=inp, d=ident[:]: E.transpose(o, i, d),
              list(ins) + [R("ident")], outs)

    def tt(out, a, b, alu, ins, outs, eng="vector"):
        PR.op(eng, lambda E, o=out, x=a, y=b, u=alu: E.tensor_tensor(o, x, y, op=u),
              ins, outs)

    def tsc(out, a, s1, op0, s2=None, op1=None, ins=(), outs=(), eng="vector"):
        if op1 is None:
            PR.op(eng, lambda E, o=out, x=a, u=s1, v=op0:
                  E.tensor_scalar(o, x, u, None, v), ins, outs)
        else:
            PR.op(eng, lambda E, o=out, x=a, u=s1, v=op0, w=s2, z=op1:
                  E.tensor_scalar(o, x, u, w, v, z), ins, outs)

    def stt(out, a, s, b, op0, op1, ins, outs, eng="vector"):
        PR.op(eng, lambda E, o=out, x=a, u=s, y=b, v=op0, w=op1:
              E.scalar_tensor_tensor(o, x, u, y, v, w), ins, outs)

    def cp(out, a, ins, outs, eng="vector"):
        if eng == "scalar":
            PR.op(eng, lambda E, o=out, x=a: E.copy(o, x), ins, outs)
        else:
            PR.op(eng, lambda E, o=out, x=a: E.tensor_copy(o, x), ins, outs)

    def act(out, a, f, bias=0.0, scale=1.0, accum=None, ins=(), outs=()):
        PR.op("scalar", lambda E, o=out, x=a, ff=f, b=bias, s=scale, au=accum:
              E.activation(o, x, ff, bias=b, scale=s, accum_out=au), ins, outs)

    def sign_act(out, a, ins, outs):
        PR.op("scalar", lambda E, o=out, x=a: E.sign(o, x), ins, outs)

    def red(out, a, op_abs=False, ins=(), outs=(), kind="sum"):
        if kind == "sum":
            PR.op("vector", lambda E, o=out, x=a, ab=op_abs:
                  E.reduce_sum(o, x, axis=AX.X, apply_absolute_value=ab), ins, outs)
        else:
            PR.op("vector", lambda E, o=out, x=a, ab=op_abs:
                  E.reduce_max(o, x, axis=AX.X, apply_absolute_value=ab), ins, outs)

    def recip(out, a, ins, outs):
        PR.op("vector", lambda E, o=out, x=a: E.reciprocal(o, x), ins, outs)

    def ld(out, a, ins, outs):
        PR.op("sync", lambda E, o=out, x=a: E.dma_start(out=o, in_=x),
              ins, outs, dma=True)

    def sd(out, a, ins, outs):
        PR.op("gpsimd", lambda E, o=out, x=a: E.dma_start(out=o, in_=x),
              ins, outs, dma=True)

    def mset(ap, v, ins, outs, eng="gpsimd"):
        PR.op(eng, lambda E, o=ap, c=v: E.memset(o, c), ins, outs)

    def scan(out, d0, ins, outs):
        PR.op("vector", lambda E, o=out, x=d0:
              E.tensor_tensor_scan(o, x, x, 0.0, ALU.add, ALU.bypass), ins, outs)

    def iota(ap, base, ins, outs):
        PR.op("gpsimd", lambda E, o=ap, b=base: E.iota(o, [[-1, 128]], base=b,
              channel_multiplier=1, allow_small_or_imprecise_dtypes=True), ins, outs)

    def nopbar(eng, ins, outs):
        PR.op(eng, lambda E: E.engine_nop(), ins, outs)

    iota(ident[:], 0, [], [R("ident")])
    tsc(ident[:], ident[:], 0.0, ALU.is_equal, ins=[R("ident")], outs=[R("ident")])
    mset(onesf[:], 1.0, [], [R("onesf")])
    iota(stq[19][:], 0, [RST[19]], [RST[19]])
    tsc(mLs[:], stq[19][:], 0.0, ALU.is_gt, ins=[RST[19]], outs=[R("mLs")])
    tsc(mUs[:], stq[19][:], 0.0, ALU.is_lt, ins=[RST[19]], outs=[R("mUs")])
    tsc(mUi[:], stq[19][:], 0.0, ALU.is_le, ins=[RST[19]], outs=[R("mUi")])
    mset(sm3[:, 0:1], 6.0, [R("sm3")], [R("sm3")])
    recip(sm3[:, 16:17], sm3[:, 0:1], [R("sm3")], [R("sm3")])
    for t in range(NT):
        ld(x_sb[:, t * C:(t + 1) * C], x_h[t * 128:(t + 1) * 128, :], [], [RX[t]])


    def layernorm(xt_ap, out_ap, rx, rout):
        red(sm1[:, 0:1], xt_ap, ins=[rx, R("sm1")], outs=[R("sm1")])
        tsc(sm1[:, 16:17], sm1[:, 0:1], -1.0 / C, ALU.mult, ins=[R("sm1")], outs=[R("sm1")])
        act(xc[:], xt_ap, AF.Identity, bias=sm1[:, 16:17],
            ins=[rx, R("sm1")], outs=[R("xc")])
        act(scrA[:, 0:C], xc[:], AF.Square, accum=sm1[:, 32:33],
            ins=[R("xc"), R("sm1")] + AQ(0, C), outs=[R("sm1")] + AQ(0, C))
        tsc(sm1[:, 48:49], sm1[:, 32:33], 1.0 / C, ALU.mult, 1e-5, ALU.add,
            ins=[R("sm1")], outs=[R("sm1")])
        recip(sm1[:, 64:65], sm1[:, 48:49], [R("sm1")], [R("sm1")])
        act(sm1[:, 80:81], sm1[:, 64:65], AF.Sqrt, ins=[R("sm1")], outs=[R("sm1")])
        tsc(out_ap, xc[:], sm1[:, 80:81], ALU.mult,
            ins=[R("xc"), R("sm1")], outs=[rout])

    def transpose_store(src_ap, rsrc, scr, nj, trow, stag):
        for j in range(nj):
            i2 = j % 2
            p = pb[6 + i2]
            tr(p[:, 0:128], src_ap[:, j * 128:(j + 1) * 128], [rsrc], [RPB[6 + i2]])
            bufi = stq[16 + i2]
            cp(bufi[:], p[:, 0:128], [RPB[6 + i2]], [RST[16 + i2]],
               eng="vector" if i2 == 0 else "scalar")
            sd(scr[j, :, trow * 128:(trow + 1) * 128], bufi[:], [RST[16 + i2]],
               [R(stag + "%d" % j)])

    def fp4q(src_ap, out_ap, rsrc, rout, n):
        nb = n // 16
        Y, P, Q, G = (scrA[:, 0:n], scrA[:, 1024:1024 + n],
                      scrA[:, 2048:2048 + n], scrA[:, 3072:3072 + n])
        rY, rP, rQ, rG = AQ(0, n), AQ(1024, 1024 + n), AQ(2048, 2048 + n), AQ(3072, 3072 + n)
        act(xc[:, 0:n], src_ap, AF.Abs, ins=[rsrc, R("xc")], outs=[R("xc")])
        red(sm1[:, 0:nb], xc[:, 0:n].rearrange("p (a b) -> p a b", b=16),
            kind="max", ins=[R("xc"), R("sm1")], outs=[R("sm1")])
        tsc(sm2[:, 0:nb], sm1[:, 0:nb], sm3[:, 16:17], ALU.mult, 1e-12, ALU.max,
            ins=[R("sm1"), R("sm3"), R("sm2")], outs=[R("sm2")])
        recip(sm1[:, 64:64 + nb], sm2[:, 0:nb], [R("sm2"), R("sm1")], [R("sm1")])
        tt(Y.rearrange("p (a b) -> p a b", b=16),
           xc[:, 0:n].rearrange("p (a b) -> p a b", b=16),
           sm1[:, 64:64 + nb].rearrange("p (a o) -> p a o", o=1).broadcast_to([128, nb, 16]),
           ALU.mult, [R("xc"), R("sm1")] + rY, rY)
        tsc(P, Y, 0.25, ALU.is_gt, ins=rY + rP, outs=rP)
        stt(Q, Y, 0.75, P, ALU.is_gt, ALU.add, rY + rP + rQ, rQ)
        stt(P, Y, 1.25, Q, ALU.is_gt, ALU.add, rY + rQ + rP, rP)
        stt(Q, Y, 1.75, P, ALU.is_gt, ALU.add, rY + rP + rQ, rQ)
        tsc(P, Q, 0.5, ALU.mult, ins=rQ + rP, outs=rP)
        stt(Q, Y, 2.5, P, ALU.is_gt, ALU.add, rY + rP + rQ, rQ)
        stt(P, Y, 3.5, Q, ALU.is_gt, ALU.add, rY + rQ + rP, rP)
        tsc(G, Y, 5.0, ALU.is_gt, 2.0, ALU.mult, ins=rY + rG, outs=rG)
        tt(Q, P, G, ALU.add, rP + rG + rQ, rQ)
        act(P, src_ap, AF.Sign, ins=[rsrc] + rP, outs=rP)
        tt(G, Q, P, ALU.mult, rQ + rP + rG, rG)
        tt(out_ap.rearrange("p (a b) -> p a b", b=16),
           G.rearrange("p (a b) -> p a b", b=16),
           sm2[:, 0:nb].rearrange("p (a o) -> p a o", o=1).broadcast_to([128, nb, 16]),
           ALU.mult, rG + [R("sm2")], [rout])

    def actq_apply(src_ap, out_ap, rsrc, rout, n, sc, rsc, rreg):
        tsc(scrA[:, 0:n], src_ap, sc, ALU.mult,
            ins=[rsrc, rreg] + AQ(0, n), outs=AQ(0, n))
        cp(ibuf[:, 0:n], scrA[:, 0:n], AQ(0, n) + [R("ibuf")], [R("ibuf")])
        cp(scrA[:, 1024:1024 + n], ibuf[:, 0:n],
           [R("ibuf")] + AQ(1024, 1024 + n), AQ(1024, 1024 + n))
        tsc(scrA[:, 0:n], scrA[:, 1024:1024 + n], -128.0, ALU.max, 127.0, ALU.min,
            ins=AQ(0, 2048), outs=AQ(0, n))
        tsc(out_ap, scrA[:, 0:n], rsc, ALU.mult,
            ins=AQ(0, n) + [rreg], outs=[rout])

    def actq_full(src_ap, out_ap, rsrc, rout, n):
        red(sm1[:, 0:1], src_ap, op_abs=True, kind="max",
            ins=[rsrc, R("sm1")], outs=[R("sm1")])
        tsc(sm1[:, 1:2], sm1[:, 0:1], 1e-5, ALU.max, ins=[R("sm1")], outs=[R("sm1")])
        recip(sm1[:, 2:3], sm1[:, 1:2], [R("sm1")], [R("sm1")])
        tsc(sm1[:, 3:4], sm1[:, 2:3], 127.0, ALU.mult, ins=[R("sm1")], outs=[R("sm1")])
        recip(sm1[:, 4:5], sm1[:, 3:4], [R("sm1")], [R("sm1")])
        actq_apply(src_ap, out_ap, rsrc, rout, n, sm1[:, 3:4], sm1[:, 4:5], R("sm1"))

    def bias_bcast(row_ap, dst, rrow, rdst, width=C):
        for off in range(0, width, 512):
            wcur = min(512, width - off)
            mm(pb[5][:, 0:wcur], onesf[0:1, 0:128], row_ap[:, off:off + wcur],
               True, True, [R("onesf"), rrow], [RPB[5]])
            cp(dst[:, off:off + wcur], pb[5][:, 0:wcur], [RPB[5], rdst], [rdst])

    def stageA(l):
        if l == 3:
            ld(rows[0:1, 0:17], W["l3_rel"][:], [], [R("rows0")])
            mm(pb[5][:, 0:17], onesf[0:1, 0:128], rows[0:1, 0:17], True, True,
               [R("onesf"), R("rows0")], [RPB[5]])
            cp(relb[:, 0:17], pb[5][:, 0:17], [RPB[5]], [R("relb")])
            for nm, base, dst, rdst in (("d8", 0, d8t, R("d8")),
                                        ("u8", -128, u8t, R("u8"))):
                iota(stq[19][:], base, [RST[19]], [RST[19]])
                act(stq[20][:], stq[19][:], AF.Abs, ins=[RST[19], RST[20]], outs=[RST[20]])
                tsc(stq[19][:], stq[20][:], 16.0, ALU.min, ins=[RST[20], RST[19]],
                    outs=[RST[19]])
                mset(stq[21][:], 0.0, [RST[21]], [RST[21]], eng="vector")
                for v in range(17):
                    tsc(stq[22][:], stq[19][:], float(v), ALU.is_equal,
                        ins=[RST[19], RST[22]], outs=[RST[22]])
                    stt(stq[21][:], stq[22][:], relb[:, v:v + 1], stq[21][:],
                        ALU.mult, ALU.add, [RST[22], R("relb"), RST[21]], [RST[21]])
                tsc(dst[:], stq[21][:], 0.125, ALU.mult, ins=[RST[21], rdst], outs=[rdst])
            tsc(sm3[:, 24:25], relb[:, 16:17], 0.125, ALU.mult,
                ins=[R("relb"), R("sm3")], outs=[R("sm3")])
        for t in range(NT):
            xt = x_sb[:, t * C:(t + 1) * C]
            layernorm(xt, xnb[:], RX[t], R("xnb"))
            if l == 0:
                tsc(mb[:, t:t + 1], xnb[:, 0:1], 0.0, ALU.is_equal, -1e30, ALU.mult,
                    ins=[R("xnb"), R("mb")], outs=[R("mb")])
            transpose_store(xnb[:], R("xnb"), xnt_s, NJ, t, "A%d_xnt_s" % l)
            if l == 0:
                fp4q(xnb[:], xqb[:], R("xnb"), R("xqb"), C)
                transpose_store(xqb[:], R("xqb"), xqt_s, NJ, t, "A%d_xqt_s" % l)
            elif l == 3:
                actq_full(xnb[:], xqb[:], R("xnb"), R("xqb"), C)
                transpose_store(xqb[:], R("xqb"), xqt_s, NJ, t, "A%d_xqt_s" % l)

    xns = scrA

    def ld_xns(l, tc):
        rr = []
        for j in range(NJ):
            q = AQ(j * 512, (j + 1) * 512)
            ld(xns[:, j * 512:(j + 1) * 512], xnt_s[j, :, tc * 512:(tc + 1) * 512],
               [R("A%d_xnt_s%d" % (l, j))] + q, q)
            rr.append(q)
        return rr

    def stageB(l, wq, wk, wv, has_bias, vdst=v_s, qdst=None, kdst=None,
               evac_sig=False):
        qdst = qt_s if qdst is None else qdst
        kdst = kt_s if kdst is None else kdst
        tgq, tgk = "B%d_q_s" % l, "B%d_k_s" % l
        if has_bias:
            ld(bqc[:], W["l%d_bq" % l][:], [R("bqc")], [R("bqc")])
            ld(bkc[:], W["l%d_bk" % l][:], [R("bkc")], [R("bkc")])
            ld(rows[0:1, :], W["l%d_bv" % l][:], [R("rows0")], [R("rows0")])
            bias_bcast(rows[0:1, :], bvb, R("rows0"), R("bvb"))
        for tc in range(4):
            rxn = ld_xns(l, tc)
            for which, wm, bcol, scr, stg in (("q", wq, bqc, qdst, tgq),
                                              ("k", wk, bkc, kdst, tgk)):
                for ot in range(NJ):
                    p = pb[ot % 4]
                    rp = RPB[ot % 4]
                    for j in range(NJ):
                        si = j % 4
                        wt = stq[si]
                        ld(wt[:], W[wm][j * 128:(j + 1) * 128, ot * 128:(ot + 1) * 128],
                           [RST[si]], [RST[si]])
                        mm(p[:, 0:512], wt[:], xns[:, j * 512:(j + 1) * 512],
                           j == 0, j == NJ - 1, [RST[si]] + rxn[j], [rp])
                    ev = pt0 if ot % 2 == 0 else pt1
                    rev = R("pt%d" % (ot % 2))
                    if has_bias:
                        act(ev[:], p[:, 0:512], AF.Identity, bias=bcol[:, ot:ot + 1],
                            ins=[rp, R("bqc" if which == "q" else "bkc"), rev],
                            outs=[rev])
                    else:
                        act(ev[:], p[:, 0:512], AF.Identity, ins=[rp, rev], outs=[rev])
                    sd(scr[ot, :, tc * 512:(tc + 1) * 512], ev[:], [rev],
                       [R(stg + "%d" % ot)])
            for j in range(NJ):
                wsl = wvf[:, (j % 2) * 1024:(j % 2) * 1024 + 1024]
                rw = WQ((j % 2) * 1024, (j % 2) * 1024 + 1024)
                ld(wsl, W[wv][j * 128:(j + 1) * 128, :], rw, rw)
                for ts_i in range(4):
                    for half in range(2):
                        bi = ts_i * 2 + half
                        mm(pb[bi][:, 0:512],
                           xns[:, j * 512 + ts_i * 128: j * 512 + ts_i * 128 + 128],
                           wsl[:, half * 512:half * 512 + 512],
                           j == 0, j == NJ - 1, rxn[j] + rw, [RPB[bi]])
            for ts_i in range(4):
                t = tc * 4 + ts_i
                for half in range(2):
                    bi = ts_i * 2 + half
                    ev = pt0 if half == 0 else pt1
                    rev = R("pt%d" % half)
                    if evac_sig:
                        act(ev[:], pb[bi][:, 0:512], AF.Sigmoid,
                            ins=[RPB[bi], rev], outs=[rev])
                    elif has_bias:
                        tt(ev[:], pb[bi][:, 0:512], bvb[:, half * 512:half * 512 + 512],
                           ALU.add, [RPB[bi], R("bvb"), rev], [rev])
                    else:
                        cp(ev[:], pb[bi][:, 0:512], [RPB[bi], rev], [rev])
                    if vdst is v_s:
                        for hh in range(8):
                            sd(v_s[half * 8 + hh, t * 128:(t + 1) * 128, :],
                               ev[:, hh * 64:(hh + 1) * 64], [rev], [R("v_s")])
                    else:
                        sd(vdst[t, :, half * 512:half * 512 + 512], ev[:], [rev],
                           [R("L2g2_%d" % t)])

    def stageC(l, causal):
        tgq, tgk = "B%d_q_s" % l, "B%d_k_s" % l
        mset(wk_a[64:128, :], 0.0, [R("wk_a")], [R("wk_a")])
        mset(wk_b[64:128, :], 0.0, [R("wk_b")], [R("wk_b")])
        mset(vg0[:].rearrange("p (t c) -> p t c", c=65)[:, :, 64:65], 1.0,
             [R("vg0")], [R("vg0")])
        for h in range(16):
            j, r0 = h // 2, (h % 2) * 64
            ld(wk_a[0:64, :], qt_s[j, r0:r0 + 64, :],
               [R(tgq + "%d" % j), R("wk_a")], [R("wk_a")])
            ld(wk_b[0:64, :], kt_s[j, r0:r0 + 64, :],
               [R(tgk + "%d" % j), R("wk_b")], [R("wk_b")])
            for t in range(NT):
                ld(vg0[:, t * 65:t * 65 + 64], v_s[h, t * 128:(t + 1) * 128, :],
                   [R("v_s"), R("vg0")], [R("vg0")])
            for qc in range(4):
                kt_hi = min(16, 4 * qc + 4) if causal else 16
                for kt in range(kt_hi):
                    ps = pb[kt % 2]
                    rps = RPB[kt % 2]
                    mm(ps[:, 0:512], wk_b[:, kt * 128:(kt + 1) * 128],
                       wk_a[:, qc * 512:(qc + 1) * 512], True, True,
                       [R("wk_a"), R("wk_b")], [rps])
                    pt = pt0 if kt % 2 == 0 else pt1
                    rpt = R("pt%d" % (kt % 2))
                    if l == 0:
                        act(pt[:], ps[:, 0:512], AF.Exp, bias=mb[:, kt:kt + 1],
                            scale=0.125, ins=[rps, R("mb"), rpt], outs=[rpt])
                    elif l == 1:
                        if kt < 4 * qc:
                            act(pt[:], ps[:, 0:512], AF.Exp, scale=0.125,
                                ins=[rps, rpt], outs=[rpt])
                        else:
                            for jj in range(4):
                                qt = 4 * qc + jj
                                blk = pt[:, jj * 128:(jj + 1) * 128]
                                pblk = ps[:, jj * 128:(jj + 1) * 128]
                                if qt < kt:
                                    mset(blk, 0.0, [rpt], [rpt], eng="vector")
                                else:
                                    act(blk, pblk, AF.Exp, scale=0.125,
                                        ins=[rps, rpt], outs=[rpt])
                                    if qt == kt:
                                        tt(blk, blk, mUi[:], ALU.mult,
                                           [rpt, R("mUi")], [rpt])
                    else:
                        for jj in range(4):
                            qt = 4 * qc + jj
                            blk = pt[:, jj * 128:(jj + 1) * 128]
                            pblk = ps[:, jj * 128:(jj + 1) * 128]
                            if qt < kt:
                                mset(blk, 0.0, [rpt], [rpt], eng="vector")
                            elif qt == kt:
                                tt(stq[15][:], pblk, d8t[:], ALU.mult,
                                   [rps, R("d8"), RST[15]], [RST[15]])
                                act(blk, stq[15][:], AF.Exp,
                                    ins=[RST[15], rpt], outs=[rpt])
                                tt(blk, blk, mUi[:], ALU.mult, [rpt, R("mUi")], [rpt])
                            elif qt == kt + 1:
                                tt(stq[15][:], pblk, u8t[:], ALU.mult,
                                   [rps, R("u8"), RST[15]], [RST[15]])
                                act(blk, stq[15][:], AF.Exp,
                                    ins=[RST[15], rpt], outs=[rpt])
                            else:
                                act(blk, pblk, AF.Exp, scale=sm3[:, 24:25],
                                    ins=[rps, R("sm3"), rpt], outs=[rpt])
                    mm(pb[2][0:65, 0:512], vg0[:, kt * 65:kt * 65 + 65], pt[:],
                       kt == 0, kt == kt_hi - 1, [R("vg0"), rpt], [RPB[2]])
                recip(rows[64:65, 0:512], pb[2][64:65, 0:512],
                      [RPB[2], R("rows_dn")], [R("rows_dn")])
                mm(pb[3][0:64, 0:512], onesf[64:65, 0:64], rows[64:65, 0:512],
                   True, True, [R("onesf"), R("rows_dn")], [RPB[3]])
                cp(bcb[0:64, :], pb[3][0:64, 0:512], [RPB[3], R("bcb")], [R("bcb")])
                tt(obuf[0:64, :], pb[2][0:64, 0:512], bcb[0:64, :], ALU.mult,
                   [RPB[2], R("bcb"), R("obuf")], [R("obuf")])
                sd(ot_s[h // 2, (h % 2) * 64:(h % 2) * 64 + 64, qc * 512:(qc + 1) * 512],
                   obuf[0:64, :], [R("obuf")], [R("C%d_ot_s%d" % (l, h // 2))])

    def stageD(l, wo, bo, post_ln):
        tgo = "C%d_ot_s" % l
        ld(rows[0:1, :], W[bo][:], [R("rows0")], [R("rows0")])
        bias_bcast(rows[0:1, :], bob, R("rows0"), R("bob"))
        if post_ln:
            ld(rows[0:1, :], W["l1_gam"][:], [R("rows0")], [R("rows0")])
            bias_bcast(rows[0:1, :], bvb, R("rows0"), R("bvb"))
            ld(rows[0:1, :], W["l1_bet"][:], [R("rows0")], [R("rows0")])
            bias_bcast(rows[0:1, :], acc0, R("rows0"), R("acc0"))
        for tgp in range(4):
            for j in range(NJ):
                otb = wk_a if j % 2 == 0 else wk_b
                rot = R("wk_a") if j % 2 == 0 else R("wk_b")
                ld(otb[:], ot_s[j, :, :], [R(tgo + "%d" % j), rot], [rot])
                wr = rhs0 if j % 2 == 0 else rhs1
                rwr = R("rhs%d" % (j % 2))
                ld(wr[:], W[wo][j * 128:(j + 1) * 128, :], [rwr], [rwr])
                for ti in range(4):
                    t = tgp * 4 + ti
                    for half in range(2):
                        mm(pb[ti * 2 + half][:, 0:512],
                           otb[:, t * 128:(t + 1) * 128],
                           wr[:, half * 512:half * 512 + 512],
                           j == 0, j == NJ - 1, [rot, rwr], [RPB[ti * 2 + half]])
            for ti in range(4):
                t = tgp * 4 + ti
                if not post_ln:
                    for half in range(2):
                        ev = pt0 if half == 0 else pt1
                        tt(ev[:], pb[ti * 2 + half][:, 0:512],
                           bob[:, half * 512:half * 512 + 512], ALU.add,
                           [RPB[ti * 2 + half], R("bob"), R("pt%d" % half)],
                           [R("pt%d" % half)])
                    for half in range(2):
                        sl = slice(t * C + half * 512, t * C + half * 512 + 512)
                        ev = pt0 if half == 0 else pt1
                        tt(x_sb[:, sl], x_sb[:, sl], ev[:], ALU.add,
                           [RX[t], R("pt%d" % half)], [RX[t]])
                else:
                    for half in range(2):
                        tt(xnb[:, half * 512:half * 512 + 512],
                           pb[ti * 2 + half][:, 0:512],
                           bob[:, half * 512:half * 512 + 512], ALU.add,
                           [RPB[ti * 2 + half], R("bob"), R("xnb")], [R("xnb")])
                    layernorm(xnb[:], xqb[:], R("xnb"), R("xqb"))
                    tt(xqb[:], xqb[:], bvb[:], ALU.mult,
                       [R("xqb"), R("bvb")], [R("xqb")])
                    tt(xqb[:], xqb[:], acc0[:], ALU.add,
                       [R("xqb"), R("acc0")], [R("xqb")])
                    tt(x_sb[:, t * C:(t + 1) * C], x_sb[:, t * C:(t + 1) * C],
                       xqb[:], ALU.add, [RX[t], R("xqb")], [RX[t]])

    def l0_ffn():
        for s in range(11):
            for wi, (wnm, coff) in enumerate((("l0_wg", 0), ("l0_wu", 256))):
                for sub in range(2):
                    ld(wk_c[:, 0:1024],
                       W[wnm][s * 256 + sub * 128:s * 256 + sub * 128 + 128, :],
                       [R("wk_c")], [R("wk_c")])
                    fp4q(wk_c[:, 0:1024], xnb[:], R("wk_c"), R("xnb"), 1024)
                    for i in range(NJ):
                        i2 = i % 2
                        tr(pb[6 + i2][:, 0:128], xnb[:, i * 128:(i + 1) * 128],
                           [R("xnb")], [RPB[6 + i2]])
                        dlo = i * 512 + coff + sub * 128
                        cp(wvf[:, dlo:dlo + 128], pb[6 + i2][:, 0:128],
                           [RPB[6 + i2]] + WQ(dlo, dlo + 128), WQ(dlo, dlo + 128),
                           eng="vector" if i2 == 0 else "scalar")
            for sub in range(8):
                ld(wk_c[:, 0:256],
                   W["l0_wd"][sub * 128:(sub + 1) * 128, s * 256:(s + 1) * 256],
                   [R("wk_c")], [R("wk_c")])
                fp4q(wk_c[:, 0:256], xnb[:, 0:256], R("wk_c"), R("xnb"), 256)
                for i in range(2):
                    tr(pb[6 + i][:, 0:128], xnb[:, i * 128:(i + 1) * 128],
                       [R("xnb")], [RPB[6 + i]])
                    cp(rh2[:, i * 1024 + sub * 128: i * 1024 + (sub + 1) * 128],
                       pb[6 + i][:, 0:128], [RPB[6 + i]] + SRWR + [R("rh2")],
                       [R("rh2")], eng="vector" if i == 0 else "scalar")
            for t in range(NT):
                for j in range(NJ):
                    si = j % 4
                    wt = stq[si]
                    ld(wt[:], xqt_s[j, :, t * 128:(t + 1) * 128],
                       [R("A0_xqt_s%d" % j), RST[si]], [RST[si]])
                    mm(pb[4][:, 0:256], wt[:], wvf[:, j * 512:j * 512 + 256],
                       j == 0, j == NJ - 1, [RST[si]] + WQ(j * 512, j * 512 + 256),
                       [RPB[4]])
                    mm(pb[5][:, 0:256], wt[:], wvf[:, j * 512 + 256:j * 512 + 512],
                       j == 0, j == NJ - 1,
                       [RST[si]] + WQ(j * 512 + 256, j * 512 + 512), [RPB[5]])
                act(pt0[:, 0:256], pb[4][:, 0:256], AF.Silu,
                    ins=[RPB[4], R("pt0")], outs=[R("pt0")])
                cp(pt1[:, 0:256], pb[5][:, 0:256], [RPB[5], R("pt1")], [R("pt1")])
                tt(obuf[:, 0:256], pt0[:, 0:256], pt1[:, 0:256], ALU.mult,
                   [R("pt0"), R("pt1"), R("obuf")], [R("obuf")])
                fp4q(obuf[:, 0:256], bcb[:, 0:256], R("obuf"), R("bcb"), 256)
                for i in range(2):
                    tr(pb[6 + i][:, 0:128], bcb[:, i * 128:(i + 1) * 128],
                       [R("bcb")], [RPB[6 + i]])
                    cp(wk_a[:, i * 128:(i + 1) * 128], pb[6 + i][:, 0:128],
                       [RPB[6 + i], R("wk_a")], [R("wk_a")])
                for half in range(2):
                    for i in range(2):
                        mm(pb[half][:, 0:512], wk_a[:, i * 128:(i + 1) * 128],
                           rh2[:, i * 1024 + half * 512: i * 1024 + half * 512 + 512],
                           i == 0, i == 1, [R("wk_a"), R("rh2")], [RPB[half]])
                for half in range(2):
                    sl = slice(t * C + half * 512, t * C + half * 512 + 512)
                    tt(x_sb[:, sl], x_sb[:, sl], pb[half][:, 0:512], ALU.add,
                       [RX[t], RPB[half]], [RX[t]])

    def l1_ffn():
        for hp in range(2):
            for jj in range(4):
                j = hp * 4 + jj
                wr = rhs0 if jj % 2 == 0 else rhs1
                rwr = R("rhs%d" % (jj % 2))
                ld(wr[:], W["l1_wf"][j * 128:(j + 1) * 128, :], [rwr], [rwr])
                sign_act(wvf[:, jj * C:(jj + 1) * C], wr[:],
                         [rwr] + WQ(jj * C, (jj + 1) * C), WQ(jj * C, (jj + 1) * C))
            for t in range(NT):
                for jj in range(4):
                    j = hp * 4 + jj
                    si = jj % 4
                    wt = stq[si]
                    ld(wt[:], xnt_s[j, :, t * 128:(t + 1) * 128],
                       [R("A1_xnt_s%d" % j), RST[si]], [RST[si]])
                    for half in range(2):
                        mm(pb[half][:, 0:512], wt[:],
                           wvf[:, jj * C + half * 512: jj * C + half * 512 + 512],
                           jj == 0, jj == 3,
                           [RST[si]] + WQ(jj * C, (jj + 1) * C), [RPB[half]])
                for half in range(2):
                    sl = slice(t * C + half * 512, t * C + half * 512 + 512)
                    tt(x_sb[:, sl], x_sb[:, sl], pb[half][:, 0:512], ALU.add,
                       [RX[t], RPB[half]], [RX[t]])

    def l3_ffn():
        scpos = {"l3_wg": 8, "l3_wu": 10, "l3_wd": 12}
        for wnm in ("l3_wg", "l3_wu", "l3_wd"):
            sp = scpos[wnm]
            mset(sm2[:, 16:17], 0.0, [R("sm2")], [R("sm2")], eng="vector")
            rows_n, cols_n = (8, 4096) if wnm != "l3_wd" else (32, 1024)
            for j in range(rows_n):
                ld(scrA[:, 0:cols_n], W[wnm][j * 128:(j + 1) * 128, :],
                   AQ(0, cols_n), AQ(0, cols_n))
                red(sm2[:, 17:18], scrA[:, 0:cols_n], op_abs=True,
                    ins=AQ(0, cols_n) + [R("sm2")], outs=[R("sm2")])
                tt(sm2[:, 16:17], sm2[:, 16:17], sm2[:, 17:18], ALU.add,
                   [R("sm2")], [R("sm2")])
            mm(pb[5][0:1, 0:1], sm2[:, 16:17], onesf[:, 0:1], True, True,
               [R("sm2"), R("onesf")], [RPB[5]])
            tsc(rows[0:1, 20:21], pb[5][0:1, 0:1], 1.0 / (4096.0 * 1024.0), ALU.mult,
                1e-5, ALU.max, ins=[RPB[5], R("rows0")], outs=[R("rows0")])
            recip(rows[0:1, 21:22], rows[0:1, 20:21], [R("rows0")], [R("rows0")])
            mm(pb[5][:, 0:1], onesf[0:1, 0:128], rows[0:1, 21:22], True, True,
               [R("rows0"), R("onesf")], [RPB[5]])
            cp(sm2[:, sp:sp + 1], pb[5][:, 0:1], [RPB[5], R("sm2")], [R("sm2")])
            mm(pb[5][:, 0:1], onesf[0:1, 0:128], rows[0:1, 20:21], True, True,
               [R("rows0"), R("onesf")], [RPB[5]])
            cp(sm2[:, sp + 1:sp + 2], pb[5][:, 0:1], [RPB[5], R("sm2")], [R("sm2")])

        def wquant(dlo, src_ap, wnm, n):
            sp = scpos[wnm]
            tsc(scrA[:, 0:n], src_ap, sm2[:, sp:sp + 1], ALU.mult,
                ins=[R("wk_c"), R("sm2")] + AQ(0, n), outs=AQ(0, n))
            cp(ibuf[:, 0:n], scrA[:, 0:n], AQ(0, n) + [R("ibuf")], [R("ibuf")])
            cp(scrA[:, 0:n], ibuf[:, 0:n], [R("ibuf")] + AQ(0, n), AQ(0, n))
            tsc(scrA[:, 0:n], scrA[:, 0:n], -1.0, ALU.max, 1.0, ALU.min,
                ins=AQ(0, n), outs=AQ(0, n))
            tsc(wvf[:, dlo:dlo + n], scrA[:, 0:n], sm2[:, sp + 1:sp + 2], ALU.mult,
                ins=AQ(0, n) + [R("sm2")] + WQ(dlo, dlo + n), outs=WQ(dlo, dlo + n))

        mset(sm1[:, 32:48], 0.0, [R("sm1")], [R("sm1")], eng="vector")
        for s in range(16):
            for wi, wnm in enumerate(("l3_wg", "l3_wu")):
                for j in range(NJ):
                    ld(wk_c[:, 0:256],
                       W[wnm][j * 128:(j + 1) * 128, s * 256:(s + 1) * 256],
                       [R("wk_c")], [R("wk_c")])
                    wquant(j * 512 + wi * 256, wk_c[:, 0:256], wnm, 256)
            for t in range(NT):
                for j in range(NJ):
                    si = j % 4
                    wt = stq[si]
                    ld(wt[:], xqt_s[j, :, t * 128:(t + 1) * 128],
                       [R("A3_xqt_s%d" % j), RST[si]], [RST[si]])
                    mm(pb[4][:, 0:256], wt[:], wvf[:, j * 512:j * 512 + 256],
                       j == 0, j == NJ - 1,
                       [RST[si]] + WQ(j * 512, j * 512 + 256), [RPB[4]])
                    mm(pb[5][:, 0:256], wt[:], wvf[:, j * 512 + 256:j * 512 + 512],
                       j == 0, j == NJ - 1,
                       [RST[si]] + WQ(j * 512 + 256, j * 512 + 512), [RPB[5]])
                act(pt0[:, 0:256], pb[4][:, 0:256], AF.Silu,
                    ins=[RPB[4], R("pt0")], outs=[R("pt0")])
                cp(pt1[:, 0:256], pb[5][:, 0:256], [RPB[5], R("pt1")], [R("pt1")])
                tt(obuf[:, 0:256], pt0[:, 0:256], pt1[:, 0:256], ALU.mult,
                   [R("pt0"), R("pt1"), R("obuf")], [R("obuf")])
                red(sm1[:, 48:49], obuf[:, 0:256], op_abs=True, kind="max",
                    ins=[R("obuf"), R("sm1")], outs=[R("sm1")])
                tt(sm1[:, 32 + t:33 + t], sm1[:, 32 + t:33 + t], sm1[:, 48:49],
                   ALU.max, [R("sm1")], [R("sm1")])
                sd(h3_s[t, :, s * 256:(s + 1) * 256], obuf[:, 0:256], [R("obuf")],
                   [R("h3_%d_%d" % (t, s))])
        for t in range(NT):
            tsc(sm1[:, 48:49], sm1[:, 32 + t:33 + t], 1e-5, ALU.max,
                ins=[R("sm1")], outs=[R("sm1")])
            recip(sm1[:, 49:50], sm1[:, 48:49], [R("sm1")], [R("sm1")])
            tsc(sm2[:, 32 + t:33 + t], sm1[:, 49:50], 127.0, ALU.mult,
                ins=[R("sm1"), R("sm2")], outs=[R("sm2")])
            recip(sm2[:, 64 + t:65 + t], sm2[:, 32 + t:33 + t],
                  [R("sm2")], [R("sm2")])
        for s in range(16):
            for j in range(2):
                ld(wk_c[:, 0:1024],
                   W["l3_wd"][s * 256 + j * 128: s * 256 + (j + 1) * 128, :],
                   [R("wk_c")], [R("wk_c")])
                wquant(j * 1024, wk_c[:, 0:1024], "l3_wd", 1024)
            for t in range(NT):
                ld(obuf[:, 0:256], h3_s[t, :, s * 256:(s + 1) * 256],
                   [R("h3_%d_%d" % (t, s)), R("obuf")], [R("obuf")])
                actq_apply(obuf[:, 0:256], bcb[:, 0:256], R("obuf"), R("bcb"), 256,
                           sm2[:, 32 + t:33 + t], sm2[:, 64 + t:65 + t], R("sm2"))
                for i in range(2):
                    tr(pb[6 + i][:, 0:128], bcb[:, i * 128:(i + 1) * 128],
                       [R("bcb")], [RPB[6 + i]])
                    cp(wk_a[:, i * 128:(i + 1) * 128], pb[6 + i][:, 0:128],
                       [RPB[6 + i], R("wk_a")], [R("wk_a")])
                for half in range(2):
                    for i in range(2):
                        mm(pb[half][:, 0:512], wk_a[:, i * 128:(i + 1) * 128],
                           wvf[:, i * 1024 + half * 512: i * 1024 + half * 512 + 512],
                           i == 0, i == 1,
                           [R("wk_a")] + WQ(i * 1024, (i + 1) * 1024), [RPB[half]])
                for half in range(2):
                    sl = slice(t * C + half * 512, t * C + half * 512 + 512)
                    tt(x_sb[:, sl], x_sb[:, sl], pb[half][:, 0:512], ALU.add,
                       [RX[t], RPB[half]], [RX[t]])

    def l2_attn_ffn():
        nopbar("vector", [R("rh2")], SRWR + [R("rh2")])
        ld(sm3[0:8, 8:9], W["l2_ba"][:], [R("sm3")], [R("sm3")])
        ld(sm3[0:8, 9:10], W["l2_bb"][:], [R("sm3")], [R("sm3")])
        for tc in range(4):
            rxn = ld_xns(2, tc)
            for which, wm, scr in (("q", "l2_wq", qraw_s), ("k", "l2_wk", kraw_s),
                                   ("v", "l2_wv", vraw_s)):
                for ot in range(NJ):
                    p = pb[ot % 4]
                    rp = RPB[ot % 4]
                    for j in range(NJ):
                        si = j % 4
                        wt = stq[si]
                        ld(wt[:], W[wm][j * 128:(j + 1) * 128, ot * 128:(ot + 1) * 128],
                           [RST[si]], [RST[si]])
                        mm(p[:, 0:512], wt[:], xns[:, j * 512:(j + 1) * 512],
                           j == 0, j == NJ - 1, [RST[si]] + rxn[j], [rp])
                    ev = pt0 if ot % 2 == 0 else pt1
                    rev = R("pt%d" % (ot % 2))
                    cp(ev[:], p[:, 0:512], [rp, rev], [rev])
                    sd(scr[ot, :, tc * 512:(tc + 1) * 512], ev[:], [rev],
                       [R("L2%s_s%d" % (which, ot))])
            for which, wm, roff in (("a", "l2_wa", 0), ("b", "l2_wb", 8)):
                p = pb[6]
                rp = RPB[6]
                for j in range(NJ):
                    si = 4 + (j % 2)
                    wt = stq[si]
                    ld(wt[:, 0:8], W[wm][j * 128:(j + 1) * 128, :], [RST[si]], [RST[si]])
                    mm(p[0:8, 0:512], wt[:, 0:8], xns[:, j * 512:(j + 1) * 512],
                       j == 0, j == NJ - 1, [RST[si]] + rxn[j], [rp])
                wdst = wk_c[0:8, tc * 512:(tc + 1) * 512] if roff == 0 else \
                    wk_c[32:40, tc * 512:(tc + 1) * 512]
                act(wdst, p[0:8, 0:512],
                    AF.Sigmoid, bias=sm3[0:8, 8 + roff // 8:9 + roff // 8],
                    ins=[rp, R("sm3"), R("wk_c")], outs=[R("wk_c")])
        for tc in range(4):
            rxn = ld_xns(2, tc)
            for j in range(NJ):
                wsl = wvf[:, (j % 2) * 1024:(j % 2) * 1024 + 1024]
                rw = WQ((j % 2) * 1024, (j % 2) * 1024 + 1024)
                ld(wsl, W["l2_wg"][j * 128:(j + 1) * 128, :], rw, rw)
                for ts_i in range(4):
                    for half in range(2):
                        bi = ts_i * 2 + half
                        mm(pb[bi][:, 0:512],
                           xns[:, j * 512 + ts_i * 128: j * 512 + ts_i * 128 + 128],
                           wsl[:, half * 512:half * 512 + 512],
                           j == 0, j == NJ - 1, rxn[j] + rw, [RPB[bi]])
            for ts_i in range(4):
                t = tc * 4 + ts_i
                for half in range(2):
                    bi = ts_i * 2 + half
                    ev = pt0 if half == 0 else pt1
                    rev = R("pt%d" % half)
                    act(ev[:], pb[bi][:, 0:512], AF.Sigmoid,
                        ins=[RPB[bi], rev], outs=[rev])
                    sd(g2_s[t, :, half * 512:half * 512 + 512], ev[:], [rev],
                       [R("L2g2_%d" % t)])
        W0 = wk_c[0:8, 0:T]
        W1 = wk_c[32:40, 0:T]
        W2 = wk_c[64:72, 0:T]
        W3 = wk_c[96:104, 0:T]
        RW = R("wk_c")

        def g_store(w_ap, row, rname):
            sd(srw[row:row + 8, 0:T], w_ap, [RW, R(rname)], [R(rname)])

        act(W2, W0, AF.Ln, ins=[RW], outs=[RW])
        tsc(W3, W1, -1.0, ALU.mult, ins=[RW], outs=[RW])
        g_store(W3, 16, "srw_nb")
        for ck in range(16):
            scan(W1[:, ck * 128:(ck + 1) * 128], W2[:, ck * 128:(ck + 1) * 128],
                 [RW], [RW])
        g_store(W1, 0, "srw_c")
        mset(W3.rearrange("p (a b) -> p a b", b=128)[:, :, 0:1], 0.0,
             [RW], [RW], eng="vector")
        cp(W3.rearrange("p (a b) -> p a b", b=128)[:, :, 1:128],
           W1.rearrange("p (a b) -> p a b", b=128)[:, :, 0:127], [RW], [RW])
        g_store(W3, 8, "srw_cp")
        act(W2, W1, AF.Exp, ins=[RW], outs=[RW])
        g_store(W2, 32, "srw_g")
        act(W0, W3, AF.Exp, ins=[RW], outs=[RW])
        g_store(W0, 40, "srw_gp")
        for ck in range(16):
            cs = ck * 128
            tsc(W2[:, cs:cs + 128], W1[:, cs:cs + 128], 0.0, ALU.mult,
                W1[:, cs + 127:cs + 128], ALU.add, ins=[RW], outs=[RW])
        act(W0, W2, AF.Exp, ins=[RW], outs=[RW])
        g_store(W0, 48, "srw_gc")
        for ck in range(16):
            cs = ck * 128
            tsc(W2[:, cs:cs + 128], W1[:, cs:cs + 128],
                W1[:, cs + 127:cs + 128], ALU.subtract, ins=[RW], outs=[RW])
        act(W0, W2, AF.Exp, scale=-1.0, ins=[RW], outs=[RW])
        g_store(W0, 24, "srw_wx")
        for which, rawscr, scr, cw, cb, extra, stag in (
                ("q", qraw_s, q2_s, "l2_cqw", "l2_cqb", None, "L2q2"),
                ("k", kraw_s, k2_s, "l2_ckw", "l2_ckb", 128.0 ** -0.5, "L2k2"),
                ("v", vraw_s, v2_s, "l2_cvw", "l2_cvb", None, "L2v2")):
            ld(bqc[:], W[cb][:], [R("bqc")], [R("bqc")])
            ld(sm2[:, 32:64], W[cw][:], [R("sm2")], [R("sm2")])
            for j in range(NJ):
                mset(wk_c[:, 0:3], 0.0, [R("wk_c")], [R("wk_c")], eng="vector")
                ld(wk_c[:, 3:3 + T], rawscr[j, :, :],
                   [R("L2%s_s%d" % (which, j)), R("wk_c")], [R("wk_c")])
                tsc(wk_a[:], wk_c[:, 0:T], sm2[:, 32 + 4 * j:33 + 4 * j], ALU.mult,
                    ins=[R("wk_c"), R("sm2"), R("wk_a")], outs=[R("wk_a")])
                for tau in range(1, 4):
                    stt(wk_a[:], wk_c[:, tau:tau + T],
                        sm2[:, 32 + 4 * j + tau:33 + 4 * j + tau],
                        wk_a[:], ALU.mult, ALU.add,
                        [R("wk_c"), R("sm2"), R("wk_a")], [R("wk_a")])
                act(wk_b[:], wk_a[:], AF.Silu, bias=bqc[:, j:j + 1],
                    ins=[R("wk_a"), R("bqc"), R("wk_b")], outs=[R("wk_b")])
                if extra is not None:
                    tsc(wk_b[:], wk_b[:], extra, ALU.mult,
                        ins=[R("wk_b")], outs=[R("wk_b")])
                for ck in range(16):
                    sd(scr[j, ck, :, :], wk_b[:, ck * 128:(ck + 1) * 128],
                       [R("wk_b")], [R(stag + "_%d_%d" % (j, ck))])
        ld(rows[0:1, 0:128], W["l2_lng"][:], [R("rows0")], [R("rows0")])
        mm(pb[5][:, 0:128], onesf[0:1, 0:128], rows[0:1, 0:128], True, True,
           [R("onesf"), R("rows0")], [RPB[5]])
        cp(d8t[:], pb[5][:, 0:128], [RPB[5], R("d8")], [R("d8")])
        ld(rows[0:1, 0:128], W["l2_lnb"][:], [R("rows0")], [R("rows0")])
        mm(pb[5][:, 0:128], onesf[0:1, 0:128], rows[0:1, 0:128], True, True,
           [R("onesf"), R("rows0")], [RPB[5]])
        cp(u8t[:], pb[5][:, 0:128], [RPB[5], R("u8")], [R("u8")])
        mset(xqb[:], 0.0, [R("xqb")], [R("xqb")], eng="vector")
        nopbar("vector", [R("acc0")], [R("m2b")])
        regsM = [R("xqb"), R("m2b")]
        for ck in range(16):
            cs = ck * 128
            mcur = xqb if ck % 2 == 0 else acc0
            mnxt = acc0 if ck % 2 == 0 else xqb
            rmc, rmn = regsM[ck % 2], regsM[(ck + 1) % 2]
            scol = scol0 if ck % 2 == 0 else scol1
            rscol = R("scol%d" % (ck % 2))
            tr(pb[5][:, 0:128], srw[0:128, cs:cs + 128], SRWR + [R("rh2")], [RPB[5]])
            cp(scol[:], pb[5][:, 0:72], [RPB[5], rscol], [rscol])
            for h in range(8):
                c_c = scol[:, 0 + h:1 + h]
                cp_c = scol[:, 8 + h:9 + h]
                nb_c = scol[:, 16 + h:17 + h]
                wx_c = scol[:, 24 + h:25 + h]
                g_c = scol[:, 32 + h:33 + h]
                gp_c = scol[:, 40 + h:41 + h]
                gc_c = scol[:, 48 + h:49 + h]
                h2 = h % 2
                kb, qb, vb2 = stq[0 + h2], stq[2 + h2], stq[4 + h2]
                rkb, rqb, rvb = RST[0 + h2], RST[2 + h2], RST[4 + h2]
                ld(kb[:], k2_s[h, ck, :, :], [R("L2k2_%d_%d" % (h, ck)), rkb], [rkb])
                ld(qb[:], q2_s[h, ck, :, :], [R("L2q2_%d_%d" % (h, ck)), rqb], [rqb])
                ld(vb2[:], v2_s[h, ck, :, :], [R("L2v2_%d_%d" % (h, ck)), rvb], [rvb])
                msl = mcur[:, h * 128:(h + 1) * 128]
                mm(pb[0][:, 0:128], kb[:], kb[:], True, True, [rkb], [RPB[0]])
                mm(pb[1][:, 0:128], kb[:], qb[:], True, True, [rkb, rqb], [RPB[1]])
                mm(pb[2][:, 0:128], kb[:], msl, True, True, [rkb, rmc], [RPB[2]])
                mm(pb[3][:, 0:128], qb[:], msl, True, True, [rqb, rmc], [RPB[3]])
                tr(pb[4][:, 0:128], vb2[:], [rvb], [RPB[4]])
                cp(stq[6][:], pb[0][:, 0:128], [RPB[0], RST[6]], [RST[6]])
                cp(stq[7][:], pb[4][:, 0:128], [RPB[4], RST[7]], [RST[7]],
                   eng="scalar")
                stt(stq[8][:], pb[2][:, 0:128], gp_c, stq[7][:],
                    ALU.mult, ALU.subtract, [RPB[2], rscol, RST[7], RST[8]], [RST[8]])
                tsc(stq[9][:], stq[8][:], nb_c, ALU.mult,
                    ins=[RST[8], rscol, RST[9]], outs=[RST[9]])
                tsc(stq[15][:], onesf[:], c_c, ALU.mult,
                    ins=[R("onesf"), rscol, RST[15]], outs=[RST[15]])
                tr(pb[4][:, 0:128], stq[15][:], [RST[15]], [RPB[4]])
                cp(stq[16][:], pb[4][:, 0:128], [RPB[4], RST[16]], [RST[16]])
                tsc(stq[15][:], onesf[:], cp_c, ALU.mult,
                    ins=[R("onesf"), rscol, RST[15]], outs=[RST[15]])
                tr(pb[4][:, 0:128], stq[15][:], [RST[15]], [RPB[4]])
                cp(stq[17][:], pb[4][:, 0:128], [RPB[4], RST[17]], [RST[17]],
                   eng="scalar")
                tsc(stq[15][:], stq[16][:], cp_c, ALU.subtract, 0.0, ALU.max,
                    ins=[RST[16], rscol, RST[15]], outs=[RST[15]])
                act(stq[11][:], stq[15][:], AF.Exp, scale=-1.0,
                    ins=[RST[15], RST[11]], outs=[RST[11]])
                tt(stq[11][:], stq[11][:], mLs[:], ALU.mult,
                   [RST[11], R("mLs")], [RST[11]])
                tt(stq[11][:], stq[11][:], stq[6][:], ALU.mult,
                   [RST[11], RST[6]], [RST[11]])
                tsc(stq[11][:], stq[11][:], nb_c, ALU.mult,
                    ins=[RST[11], rscol], outs=[RST[11]])
                tsc(stq[15][:], stq[17][:], c_c, ALU.subtract, 0.0, ALU.min,
                    ins=[RST[17], rscol, RST[15]], outs=[RST[15]])
                act(stq[13][:], stq[15][:], AF.Exp, ins=[RST[15], RST[13]],
                    outs=[RST[13]])
                tt(stq[13][:], stq[13][:], mUs[:], ALU.mult,
                   [RST[13], R("mUs")], [RST[13]])
                tt(stq[13][:], stq[13][:], stq[6][:], ALU.mult,
                   [RST[13], RST[6]], [RST[13]])
                tsc(stq[15][:], onesf[:], nb_c, ALU.mult,
                    ins=[R("onesf"), rscol, RST[15]], outs=[RST[15]])
                tr(pb[4][:, 0:128], stq[15][:], [RST[15]], [RPB[4]])
                tt(stq[13][:], stq[13][:], pb[4][:, 0:128], ALU.mult,
                   [RST[13], RPB[4]], [RST[13]])
                xi, ni, nti = 9, 11, 13
                for jl in range(7):
                    mm(pb[0][:, 0:128], stq[nti][:], stq[xi][:], True, True,
                       [RST[nti], RST[xi]], [RPB[0]])
                    xn = 19 - xi
                    tt(stq[xn][:], pb[0][:, 0:128], stq[xi][:], ALU.add,
                       [RPB[0], RST[xi], RST[xn]], [RST[xn]])
                    if jl < 6:
                        mm(pb[2][:, 0:128], stq[nti][:], stq[ni][:], True, True,
                           [RST[nti], RST[ni]], [RPB[2]])
                        mm(pb[4][:, 0:128], stq[ni][:], stq[nti][:], True, True,
                           [RST[ni], RST[nti]], [RPB[4]])
                        nn, ntn = 23 - ni, 27 - nti
                        cp(stq[nn][:], pb[2][:, 0:128], [RPB[2], RST[nn]], [RST[nn]])
                        cp(stq[ntn][:], pb[4][:, 0:128], [RPB[4], RST[ntn]],
                           [RST[ntn]], eng="scalar")
                        ni, nti = nn, ntn
                    xi = xn
                U, rU = stq[xi], RST[xi]
                tsc(stq[15][:], stq[16][:], c_c, ALU.subtract, 0.0, ALU.min,
                    ins=[RST[16], rscol, RST[15]], outs=[RST[15]])
                act(stq[18][:], stq[15][:], AF.Exp, ins=[RST[15], RST[18]],
                    outs=[RST[18]])
                tt(stq[18][:], stq[18][:], mUi[:], ALU.mult,
                   [RST[18], R("mUi")], [RST[18]])
                tt(stq[18][:], pb[1][:, 0:128], stq[18][:], ALU.mult,
                   [RPB[1], RST[18]], [RST[18]])
                tsc(stq[8][:], pb[3][:, 0:128], g_c, ALU.mult,
                    ins=[RPB[3], rscol, RST[8]], outs=[RST[8]])
                mm(pb[3][:, 0:128], stq[18][:], U[:], True, True,
                   [RST[18], rU], [RPB[3]])
                tt(stq[19][:], pb[3][:, 0:128], stq[8][:], ALU.add,
                   [RPB[3], RST[8], RST[19]], [RST[19]])
                tr(pb[4][:, 0:128], kb[:], [rkb], [RPB[4]])
                tsc(stq[18][:], pb[4][:, 0:128], wx_c, ALU.mult,
                    ins=[RPB[4], rscol, RST[18]], outs=[RST[18]])
                mm(pb[2][:, 0:128], stq[18][:], U[:], True, True,
                   [RST[18], rU], [RPB[2]])
                stt(mnxt[:, h * 128:(h + 1) * 128], msl, gc_c, pb[2][:, 0:128],
                    ALU.mult, ALU.add, [rmc, rscol, RPB[2], rmn], [rmn])
                red(sm1[:, 0:1], stq[19][:], ins=[RST[19], R("sm1")], outs=[R("sm1")])
                tsc(sm1[:, 16:17], sm1[:, 0:1], -1.0 / 128.0, ALU.mult,
                    ins=[R("sm1")], outs=[R("sm1")])
                act(stq[20][:], stq[19][:], AF.Identity, bias=sm1[:, 16:17],
                    ins=[RST[19], R("sm1"), RST[20]], outs=[RST[20]])
                act(stq[15][:], stq[20][:], AF.Square, accum=sm1[:, 32:33],
                    ins=[RST[20], R("sm1"), RST[15]], outs=[R("sm1"), RST[15]])
                tsc(sm1[:, 48:49], sm1[:, 32:33], 1.0 / 128.0, ALU.mult, 1e-5, ALU.add,
                    ins=[R("sm1")], outs=[R("sm1")])
                recip(sm1[:, 64:65], sm1[:, 48:49], [R("sm1")], [R("sm1")])
                act(sm1[:, 80:81], sm1[:, 64:65], AF.Sqrt,
                    ins=[R("sm1")], outs=[R("sm1")])
                tsc(stq[20][:], stq[20][:], sm1[:, 80:81], ALU.mult,
                    ins=[RST[20], R("sm1")], outs=[RST[20]])
                tt(stq[20][:], stq[20][:], d8t[:], ALU.mult,
                   [RST[20], R("d8")], [RST[20]])
                tt(stq[20][:], stq[20][:], u8t[:], ALU.add,
                   [RST[20], R("u8")], [RST[20]])
                ld(stq[21][:], g2_s[ck, :, h * 128:(h + 1) * 128],
                   [R("L2g2_%d" % ck), RST[21]], [RST[21]])
                tt(stq[20][:], stq[20][:], stq[21][:], ALU.mult,
                   [RST[20], RST[21]], [RST[20]])
                tr(pb[4][:, 0:128], stq[20][:], [RST[20]], [RPB[4]])
                cp(stq[22][:], pb[4][:, 0:128], [RPB[4], RST[22]], [RST[22]])
                sd(ot_s[h, :, cs:cs + 128], stq[22][:], [RST[22]],
                   [R("C2_ot_s%d" % h)])
        for tgp in range(4):
            for j in range(NJ):
                otb = wk_a if j % 2 == 0 else wk_b
                rot = R("wk_a") if j % 2 == 0 else R("wk_b")
                ld(otb[:], ot_s[j, :, :], [R("C2_ot_s%d" % j), rot], [rot])
                wr = rhs0 if j % 2 == 0 else rhs1
                rwr = R("rhs%d" % (j % 2))
                ld(wr[:], W["l2_wo"][j * 128:(j + 1) * 128, :], [rwr], [rwr])
                for ti in range(4):
                    t = tgp * 4 + ti
                    for half in range(2):
                        mm(pb[ti * 2 + half][:, 0:512], otb[:, t * 128:(t + 1) * 128],
                           wr[:, half * 512:half * 512 + 512],
                           j == 0, j == NJ - 1, [rot, rwr], [RPB[ti * 2 + half]])
            for ti in range(4):
                t = tgp * 4 + ti
                for half in range(2):
                    sl = slice(t * C + half * 512, t * C + half * 512 + 512)
                    tt(x_sb[:, sl], x_sb[:, sl], pb[ti * 2 + half][:, 0:512],
                       ALU.add, [RX[t], RPB[ti * 2 + half]], [RX[t]])
        moe()

    def moe():
        ld(rh2[0:8, 0:C], W["l2_be"][:], [R("rh2")], SRWR + [R("rh2")])
        for t in range(NT):
            p = pb[4]
            rp = RPB[4]
            for j in range(NJ):
                si = j % 2
                lt = stq[si]
                ld(lt[:], xnt_s[j, :, t * 128:(t + 1) * 128],
                   [R("A2_xnt_s%d" % j), RST[si]], [RST[si]])
                wt = stq[4 + si]
                ld(wt[:, 0:8], W["l2_mwg"][j * 128:(j + 1) * 128, :],
                   [RST[4 + si]], [RST[4 + si]])
                mm(p[:, 0:8], lt[:], wt[:, 0:8], j == 0, j == NJ - 1,
                   [RST[si], RST[4 + si]], [rp])
            cp(sm1[:, 16:24], p[:, 0:8], [rp, R("sm1")], [R("sm1")])
            red(sm1[:, 0:1], sm1[:, 16:24], kind="max",
                ins=[R("sm1")], outs=[R("sm1")])
            tsc(sm1[:, 32:40], sm1[:, 16:24], sm1[:, 0:1], ALU.is_equal,
                ins=[R("sm1")], outs=[R("sm1")])
            stt(sm1[:, 40:48], sm1[:, 32:40], -1e30, sm1[:, 16:24],
                ALU.mult, ALU.add, [R("sm1")], [R("sm1")])
            red(sm1[:, 48:49], sm1[:, 40:48], kind="max",
                ins=[R("sm1")], outs=[R("sm1")])
            tt(sm1[:, 52:53], sm1[:, 0:1], sm1[:, 48:49], ALU.subtract,
               [R("sm1")], [R("sm1")])
            act(sm1[:, 96:97], sm1[:, 52:53], AF.Sigmoid,
                ins=[R("sm1")], outs=[R("sm1")])
            act(sm1[:, 112:113], sm1[:, 52:53], AF.Sigmoid, scale=-1.0,
                ins=[R("sm1")], outs=[R("sm1")])
            tsc(sm1[:, 56:64], sm1[:, 40:48], sm1[:, 48:49], ALU.is_equal,
                ins=[R("sm1")], outs=[R("sm1")])
            tsc(sm1[:, 32:40], sm1[:, 32:40], sm1[:, 96:97], ALU.mult,
                ins=[R("sm1")], outs=[R("sm1")])
            stt(scrA[:, 2048 + t * 8:2048 + t * 8 + 8], sm1[:, 56:64],
                sm1[:, 112:113], sm1[:, 32:40], ALU.mult, ALU.add,
                [R("sm1")] + AQ(2048, 3072), AQ(2048, 3072))
            tr(pb[5][0:8, 0:128], scrA[:, 2048 + t * 8:2048 + t * 8 + 8],
               AQ(2048, 3072), [RPB[5]])
            cp(scrA[0:8, t * 128:(t + 1) * 128], pb[5][0:8, 0:128],
               [RPB[5]] + AQ(0, 2048), AQ(0, 2048))
        for tgp in range(4):
            for ti in range(4):
                t = tgp * 4 + ti
                for half in range(2):
                    mm(pb[ti * 2 + half][:, 0:512], scrA[0:8, t * 128:(t + 1) * 128],
                       rh2[0:8, half * 512:half * 512 + 512], True, True,
                       AQ(0, 2048) + [R("rh2")], [RPB[ti * 2 + half]])
                for half in range(2):
                    wlo = ti * 1024 + half * 512
                    cp(wvf[:, wlo:wlo + 512], pb[ti * 2 + half][:, 0:512],
                       [RPB[ti * 2 + half]] + WQ(wlo, wlo + 512), WQ(wlo, wlo + 512))
            for e in range(8):
                for j in range(NJ):
                    wr = rhs0 if j % 2 == 0 else rhs1
                    rwr = R("rhs%d" % (j % 2))
                    ld(wr[:], W["l2_we"][e, j * 128:(j + 1) * 128, :], [rwr], [rwr])
                    for ti in range(4):
                        t = tgp * 4 + ti
                        si = 8 + ti
                        lt = stq[si]
                        ld(lt[:], xnt_s[j, :, t * 128:(t + 1) * 128],
                           [R("A2_xnt_s%d" % j), RST[si]], [RST[si]])
                        for half in range(2):
                            mm(pb[ti * 2 + half][:, 0:512], lt[:],
                               wr[:, half * 512:half * 512 + 512],
                               j == 0, j == NJ - 1, [RST[si], rwr],
                               [RPB[ti * 2 + half]])
                for ti in range(4):
                    t = tgp * 4 + ti
                    for half in range(2):
                        wlo = ti * 1024 + half * 512
                        asl = wvf[:, wlo:wlo + 512]
                        stt(asl, pb[ti * 2 + half][:, 0:512],
                            scrA[:, 2048 + t * 8 + e:2048 + t * 8 + e + 1], asl,
                            ALU.mult, ALU.add,
                            [RPB[ti * 2 + half]] + AQ(2048, 3072) + WQ(wlo, wlo + 512),
                            WQ(wlo, wlo + 512))
            for ti in range(4):
                t = tgp * 4 + ti
                for half in range(2):
                    sl = slice(t * C + half * 512, t * C + half * 512 + 512)
                    wlo = ti * 1024 + half * 512
                    tt(x_sb[:, sl], x_sb[:, sl], wvf[:, wlo:wlo + 512],
                       ALU.add, [RX[t]] + WQ(wlo, wlo + 512), [RX[t]])

    for l in layers:
        if l == 0:
            stageA(0)
            stageB(0, "l0_wq", "l0_wk", "l0_wv", True)
            stageC(0, False)
            stageD(0, "l0_wo", "l0_bo", False)
            l0_ffn()
        elif l == 1:
            stageA(1)
            stageB(1, "l1_wq", "l1_wk", "l1_wv", True)
            stageC(1, True)
            stageD(1, "l1_wo", "l1_bo", True)
            l1_ffn()
        elif l == 2:
            stageA(2)
            l2_attn_ffn()
        elif l == 3:
            stageA(3)
            stageB(3, "l3_wq", "l3_wk", "l3_wv", False)
            stageC(3, True)
            stageD(3, "l3_wo", "l3_bo", False)
            l3_ffn()

    out_regs = []
    for t in range(NT):
        ro = R("out%d" % t)
        sd(out_h[t * 128:(t + 1) * 128, :], x_sb[:, t * C:(t + 1) * C], [RX[t]], [ro])
        out_regs.append(ro)
    PR.op("gpsimd", lambda E: E.engine_nop(), out_regs, [])

    sems = {}
    with st:
        for e in ENGS:
            sems[("c", e)] = st.enter_context(nc.semaphore("sem_c_%s" % e))
        for q in ("sync", "gpsimd"):
            for s in range(NSLOT):
                sems[("ds", q, s)] = st.enter_context(
                    nc.semaphore("sem_d_%s_%d" % (q, s)))
        with nc.Block() as block:
            @block.tensor
            def _(E):
                PR.replay("tensor", E, sems)

            @block.vector
            def _(E):
                PR.replay("vector", E, sems)

            @block.scalar
            def _(E):
                PR.replay("scalar", E, sems)

            @block.gpsimd
            def _(E):
                PR.replay("gpsimd", E, sems)

            @block.sync
            def _(E):
                PR.replay("sync", E, sems)
    return nc



def _prep(inputs):
    f = np.ascontiguousarray
    d = {}
    d["l0_wq"] = f(inputs["L0_attn_wq_w"].T)
    d["l0_bq"] = f(inputs["L0_attn_wq_b"].reshape(8, 128).T)
    d["l0_wk"] = f(inputs["L0_attn_wk_w"].T)
    d["l0_bk"] = f(inputs["L0_attn_wk_b"].reshape(8, 128).T)
    d["l0_wv"] = f(inputs["L0_attn_wv_w"].T)
    d["l0_bv"] = f(inputs["L0_attn_wv_b"].reshape(1, C))
    d["l0_wo"] = f(inputs["L0_attn_wo_w"].T)
    d["l0_bo"] = f(inputs["L0_attn_wo_b"].reshape(1, C))
    d["l0_wg"] = f(inputs["L0_ffn_w_gate"])
    d["l0_wu"] = f(inputs["L0_ffn_w_up"])
    d["l0_wd"] = f(inputs["L0_ffn_w_down"])
    d["l1_wq"] = f(inputs["L1_attn_Wq"])
    d["l1_bq"] = f(inputs["L1_attn_bq"].reshape(8, 128).T)
    d["l1_wk"] = f(inputs["L1_attn_Wk"])
    d["l1_bk"] = f(inputs["L1_attn_bk"].reshape(8, 128).T)
    d["l1_wv"] = f(inputs["L1_attn_Wv"])
    d["l1_bv"] = f(inputs["L1_attn_bv"].reshape(1, C))
    d["l1_wo"] = f(inputs["L1_attn_Wo"])
    d["l1_bo"] = f(inputs["L1_attn_bo"].reshape(1, C))
    d["l1_gam"] = f(inputs["L1_attn_gamma"].reshape(1, C))
    d["l1_bet"] = f(inputs["L1_attn_beta"].reshape(1, C))
    d["l1_wf"] = f(inputs["L1_ffn_weight"].T)
    d["l2_wq"] = f(inputs["L2_attn_Wq"])
    d["l2_wk"] = f(inputs["L2_attn_Wk"])
    d["l2_wv"] = f(inputs["L2_attn_Wv"])
    for nm, src in (("q", "L2_attn_conv_q"), ("k", "L2_attn_conv_k"),
                    ("v", "L2_attn_conv_v")):
        d["l2_c%sw" % nm] = f(inputs[src + "_w"].reshape(1024, 4).reshape(8, 128, 4)
                              .transpose(1, 0, 2).reshape(128, 32))
        d["l2_c%sb" % nm] = f(inputs[src + "_b"].reshape(8, 128).T)
    d["l2_wa"] = f(inputs["L2_attn_Wa"])
    d["l2_ba"] = f(inputs["L2_attn_ba"].reshape(8, 1))
    d["l2_wb"] = f(inputs["L2_attn_Wb"])
    d["l2_bb"] = f(inputs["L2_attn_bb"].reshape(8, 1))
    d["l2_wg"] = f(inputs["L2_attn_Wg"])
    d["l2_wo"] = f(inputs["L2_attn_Wo"])
    d["l2_lng"] = f(inputs["L2_attn_ln_g"].reshape(1, 128))
    d["l2_lnb"] = f(inputs["L2_attn_ln_b"].reshape(1, 128))
    d["l2_mwg"] = f(inputs["L2_ffn_Wg"].T)
    d["l2_we"] = f(inputs["L2_ffn_We"].transpose(0, 2, 1))
    d["l2_be"] = f(inputs["L2_ffn_be"])
    d["l3_wq"] = f(inputs["L3_attn_Wq"])
    d["l3_wk"] = f(inputs["L3_attn_Wk"])
    d["l3_wv"] = f(inputs["L3_attn_Wv"])
    d["l3_wo"] = f(inputs["L3_attn_Wo"])
    d["l3_bo"] = f(inputs["L3_attn_bo"].reshape(1, C))
    d["l3_rel"] = f(inputs["L3_attn_rel_emb"].reshape(1, 17))
    d["l3_wg"] = f(inputs["L3_ffn_w_gate"].T)
    d["l3_wu"] = f(inputs["L3_ffn_w_up"].T)
    d["l3_wd"] = f(inputs["L3_ffn_w_down"].T)
    return d


def kernel(**inputs):
    x = np.ascontiguousarray(inputs["L0_attn_x"])
    wd = _prep(inputs)
    nc = build_program(LAYERS)
    in_maps = []
    for b in range(8):
        m = dict(wd)
        m["x"] = np.ascontiguousarray(x[b])
        in_maps.append(m)
    res = run_bass_kernel_spmd(nc, in_maps, core_ids=list(range(8)))
    return np.stack([r["out"] for r in res.results], axis=0)
